# version 34
# baseline (speedup 1.0000x reference)
import numpy as np
import ml_dtypes
from contextlib import ExitStack
import concourse.bass as bass
import concourse.mybir as mybir
from concourse.bass_utils import run_bass_kernel_spmd

F32 = mybir.dt.float32
BF16 = mybir.dt.bfloat16
AF = mybir.ActivationFunctionType
OP = mybir.AluOpType
AX = mybir.AxisListType
BF = ml_dtypes.bfloat16

D = 1024
HF = 4096
NH = 16
S = 2048
NB = 4
NCH = 8
HCH = 32
TG = 512
NGRP = S // TG
NTIL = TG // 128
NSLOT = 4
EPS_LN = 1e-5
EPS_AT = 1e-6
BIG = 1e9


class _B:
    def __init__(self, nb=NB):
        self.nb = nb
        self.nc = bass.Bass("TRN2", target_bir_lowering=False)
        self.cnt = {}
        self.lastw = {}
        self.base = {}
        self.sem = {}
        self.ev = {}

    def inc(self, inst, s, n=1):
        inst.then_inc(self.sem[s], n)
        self.cnt[s] += n
        return self.cnt[s]

    def w(self, eng, s, c=None):
        if c is None:
            c = self.cnt[s]
        if c <= 0 or c <= self.lastw.get((eng.engine, s), -1):
            return
        if self.base.get(s) is None:
            eng.wait_ge(self.sem[s], c)
        else:
            tmp = self.wtmp[eng.engine]
            eng.reg_alu(tmp, self.regs[s][eng.engine], c, op=OP.add)
            eng.wait_ge(self.sem[s], tmp)
        self.lastw[(eng.engine, s)] = c

    def setev(self, key, sem, cnt):
        self.ev[key] = (sem, cnt)

    def wk(self, eng, key):
        e = self.ev.get(key)
        if e is not None:
            self.w(eng, e[0], e[1])

    def qfence(self, s):
        eng = {"in": self.nc.sync, "wt": self.nc.gpsimd, "st": self.nc.scalar}[s]
        self.w(eng, s)

    def barrier(self):
        nc = self.nc
        self.w(nc.sync, "in")
        self.w(nc.gpsimd, "wt")
        self.w(nc.scalar, "st")
        nc.all_engine_barrier()
        for s in self.sem:
            for eng in (nc.sync, nc.gpsimd, nc.scalar, nc.vector, nc.tensor):
                k = (eng.engine, s)
                self.lastw[k] = max(self.lastw.get(k, -1), self.cnt[s])

    def build(self):
        nc = self.nc
        nb = self.nb
        dx = nc.dram_tensor("x", [nb, S, D], F32, kind="ExternalInput").ap()
        dout = nc.dram_tensor("out", [nb, S, D], F32, kind="ExternalOutput").ap()
        dxr = nc.dram_tensor("xr", [nb, S, D], F32, kind="Internal").ap()
        dqt = nc.dram_tensor("qt", [NGRP, 128, NCH * TG], BF16, kind="Internal").ap()
        dkr = nc.dram_tensor("kr", [S, D], BF16, kind="Internal").ap()
        dxnt = nc.dram_tensor("xnt_s", [NGRP, 128, NCH * TG], BF16, kind="Internal").ap()
        dw1t = nc.dram_tensor("w1t", [D, HF], F32, kind="ExternalInput").ap()
        dw2t = nc.dram_tensor("w2t", [HF, D], F32, kind="ExternalInput").ap()
        dwq = nc.dram_tensor("wq", [D, D], F32, kind="ExternalInput").ap()
        dwk = nc.dram_tensor("wk", [D, D], F32, kind="ExternalInput").ap()
        dwv = nc.dram_tensor("wv", [D, D], F32, kind="ExternalInput").ap()
        dwo = nc.dram_tensor("wo", [D, D], F32, kind="ExternalInput").ap()
        drwt = nc.dram_tensor("rwt", [D, NH], F32, kind="ExternalInput").ap()
        drb = nc.dram_tensor("rb", [NH], F32, kind="ExternalInput").ap()
        db1 = nc.dram_tensor("b1", [HF], F32, kind="ExternalInput").ap()
        db2 = nc.dram_tensor("b2", [D], F32, kind="ExternalInput").ap()
        dident = nc.dram_tensor("ident", [128, 128], BF16, kind="ExternalInput").ap()
        dcos = nc.dram_tensor("cost", [16, 128, 32], F32, kind="ExternalInput").ap()
        dsin = nc.dram_tensor("sint", [16, 128, 32], F32, kind="ExternalInput").ap()
        dcosl = nc.dram_tensor("costl", [16, 128, 32], F32, kind="ExternalInput").ap()
        dsinl = nc.dram_tensor("sintl", [16, 128, 32], F32, kind="ExternalInput").ap()
        dind = nc.dram_tensor("indc", [NH, NCH * 128], BF16, kind="ExternalInput").ap()

        es = ExitStack()
        self.es = es
        sbt = lambda n, shp, dt: es.enter_context(nc.sbuf_tensor(n, shp, dt))
        pst = lambda n, shp, dt: es.enter_context(nc.psum_tensor(n, shp, dt))

        w1s = sbt("w1s", [128, NCH * HF], BF16)
        w2s = sbt("w2s", [128, HCH * D], BF16)
        arena = sbt("arena", [128, 17920], BF16)
        xbuf = sbt("xbuf", [128, 5 * D], F32)
        abuf = sbt("abuf", [128, 2048], BF16)
        xnt = sbt("xnt", [128, NCH * TG], BF16)
        xn = sbt("xn", [128, 4 * D], BF16)
        coss = sbt("coss", [128, 16 * 32], BF16)
        sins = sbt("sins", [128, 16 * 32], BF16)
        rwts = sbt("rwts", [128, NCH * NH], BF16)
        b1s = sbt("b1s", [128, HCH], F32)
        idents = sbt("idents", [128, 128], BF16)
        ones1 = sbt("ones1", [1, 128], BF16)
        onesc = sbt("onesc", [128, 1], BF16)
        bds = sbt("bds", [128, NCH * NH], BF16)
        stats = sbt("stats", [128, 8 * NTIL], F32)
        max8 = sbt("max8", [128, 8], F32)
        penb = sbt("penb", [128, 128], F32)
        penb16 = sbt("penb16", [128, 128], BF16)
        rbs = sbt("rbs", [128, NH], F32)

        a_hT = arena[:, 0:HCH * TG]
        a_b2 = arena[0:1, 16384:16384 + D]
        a_pent = arena[64:80, 17408:17408 + TG]
        a_wcur = arena[:, 0:8192]
        a_wo = arena[:, 0:8192]
        a_m = arena[:, 8192:16384]
        a_rd = arena[64:80, 0:S]
        a_ind = arena[64:80, 16384:16384 + NCH * 128]

        PT = pst("PT", [128, 1024], BF16)
        PL = pst("PL", [128, 512], F32)
        PA = pst("PA", [128, 1024], F32)
        PB = pst("PB", [128, 1024], F32)
        PC = pst("PC", [128, 1024], F32)

        for s in ("in", "wt", "st", "T", "V", "A", "G"):
            self.sem[s] = es.enter_context(nc.semaphore("sem_" + s))
            self.cnt[s] = 0

        with nc.Block() as blk:
            @blk.sync
            def _(sync):
                pass

        ld = nc.sync.dma_start

        def gld(**kw):
            self.qfence("wt")
            return nc.gpsimd.dma_start(**kw)

        def chunk_ap(dap, nk, j, k0=0):
            return bass.AP(dap.tensor, k0 * 128 * j, [[j, 128], [128 * j, nk], [1, j]])

        self.inc(ld(out=idents[:], in_=dident), "in", 16)
        self.inc(ld(out=b1s[:], in_=db1.rearrange("(c p) -> p c", p=128),
                    allow_slow_non_contiguous=True), "in", 16)
        self.inc(ld(out=rbs[:], in_=drb.unsqueeze(0).broadcast_to([128, NH])), "in", 16)
        for h in range(2):
            self.inc(gld(out=w1s[:, h * 4 * HF:(h + 1) * 4 * HF],
                         in_=chunk_ap(dw1t, 4, HF, k0=h * 4)), "wt", 16)
        for h in range(4):
            self.inc(gld(out=w2s[:, h * 8 * D:(h + 1) * 8 * D],
                         in_=chunk_ap(dw2t, 8, D, k0=h * 8)), "wt", 16)
        self.inc(gld(out=rwts[:], in_=chunk_ap(drwt, NCH, NH)), "wt", 16)
        self.inc(gld(out=a_ind, in_=dind), "wt", 16)
        self.inc(gld(out=coss[:], in_=dcos.transpose([1, 0, 2])), "wt", 16)
        self.inc(gld(out=sins[:], in_=dsin.transpose([1, 0, 2])), "wt", 16)
        self.inc(gld(out=coss[:], in_=dcosl.transpose([1, 0, 2]),
                     accum_op=OP.add), "wt", 16)
        self.inc(gld(out=sins[:], in_=dsinl.transpose([1, 0, 2]),
                     accum_op=OP.add), "wt", 16)
        self.inc(nc.vector.memset(penb[:], 0.0), "V")
        nc.vector.drain()
        self.inc(nc.vector.memset(penb16[:], 0.0), "V")
        nc.vector.drain()
        self.inc(nc.vector.memset(ones1[:], 1.0), "V")
        nc.vector.drain()
        self.inc(nc.vector.memset(onesc[:], 1.0), "V")
        self.barrier()

        regs = {s: nc.alloc_registers("base_" + s, engines=mybir.ALL_ENGINES)
                for s in self.sem}
        self.regs = regs
        for s in self.sem:
            nc.regs_mov(regs[s], self.cnt[s])
        self.wtmp = {}
        for eng in (nc.sync, nc.gpsimd, nc.scalar, nc.vector, nc.tensor):
            self.wtmp[eng.engine] = eng.alloc_register("wtmp")

        self.slot_evt = {}
        fori_cm = nc.Fori(0, nb)
        bi = fori_cm.__enter__()
        for s in self.sem:
            self.base[s] = True
            self.cnt[s] = 0
        self.lastw = {}
        self.ev = {}

        def tok0(g, t):
            return g * TG + t * 128

        def wpairs(eng, cin):
            if isinstance(cin[0], tuple):
                for qq, cc in cin:
                    self.w(eng, qq, cc)
            else:
                self.w(eng, cin[0], cin[1])

        def ln_heavy(slot, t, cIn, sqout=None):
            xa = xbuf[:, slot * D:(slot + 1) * D]
            st = stats[:, t * 8:t * 8 + 8]
            wpairs(nc.scalar, cIn)
            if sqout is None:
                self.wk(nc.scalar, ("xnrd", t))
                sqout = xn[:, t * D:(t + 1) * D]
            nc.scalar.drain()
            self.inc(nc.scalar.activation(sqout, xa, AF.Copy,
                                          accum_out=st[:, 0:1]), "A")
            nc.scalar.drain()
            cA = self.inc(nc.scalar.activation(sqout, xa, AF.Square,
                                               accum_out=st[:, 1:2]), "A")
            return cA

        def ln_smalls(ts_):
            n = len(ts_)
            o = ts_[0] * 8
            if n == 1:
                st = stats[:, o:o + 8]
                nc.vector.drain()
                nc.vector.tensor_scalar(st[:, 2:3], st[:, 0:1], 1.0 / D, None, OP.mult)
                nc.vector.drain()
                nc.vector.tensor_scalar(st[:, 3:4], st[:, 2:3], st[:, 2:3], EPS_LN,
                                        OP.mult, OP.subtract)
                nc.vector.drain()
                nc.vector.tensor_scalar(st[:, 4:5], st[:, 1:2], 1.0 / D, st[:, 3:4],
                                        OP.mult, OP.subtract)
                nc.vector.drain()
                cV = self.inc(nc.vector.reciprocal(st[:, 5:6], st[:, 4:5]), "V")
                self.w(nc.scalar, "V", cV)
                nc.scalar.drain()
                return self.inc(nc.scalar.activation(st[:, 6:7], st[:, 5:6],
                                                     AF.Sqrt), "A")
            mr = bass.AP(stats, o + 0, [[8 * NTIL, 128], [8, n], [1, 1]])
            sq = bass.AP(stats, o + 1, [[8 * NTIL, 128], [8, n], [1, 1]])
            m_ = bass.AP(stats, o + 2, [[8 * NTIL, 128], [8, n], [1, 1]])
            msq = bass.AP(stats, o + 3, [[8 * NTIL, 128], [8, n], [1, 1]])
            vpe = bass.AP(stats, o + 4, [[8 * NTIL, 128], [8, n], [1, 1]])
            rv = bass.AP(stats, o + 5, [[8 * NTIL, 128], [8, n], [1, 1]])
            rstd = bass.AP(stats, o + 6, [[8 * NTIL, 128], [8, n], [1, 1]])
            nc.vector.drain()
            nc.vector.tensor_scalar(m_, mr, 1.0 / D, None, OP.mult)
            nc.vector.drain()
            nc.vector.tensor_tensor(msq, m_, m_, OP.mult)
            nc.vector.drain()
            nc.vector.tensor_scalar(msq, msq, EPS_LN, None, OP.subtract)
            nc.vector.drain()
            nc.vector.tensor_scalar(vpe, sq, 1.0 / D, None, OP.mult)
            nc.vector.drain()
            nc.vector.tensor_tensor(vpe, vpe, msq, OP.subtract)
            nc.vector.drain()
            cV = self.inc(nc.vector.reciprocal(rv, vpe), "V")
            self.w(nc.scalar, "V", cV)
            nc.scalar.drain()
            return self.inc(nc.scalar.activation(rstd, rv, AF.Sqrt), "A")

        def ln_xn(slot, t, cA, eng="V"):
            xa = xbuf[:, slot * D:(slot + 1) * D]
            st = stats[:, t * 8:t * 8 + 8]
            xns = xn[:, t * D:(t + 1) * D]
            if eng == "G":
                self.w(nc.gpsimd, "A", cA)
                self.wk(nc.gpsimd, ("xnrd", t))
                nc.gpsimd.drain()
                cG = self.inc(nc.gpsimd.tensor_scalar(
                    xns, xa, st[:, 2:3], st[:, 6:7], OP.subtract, OP.mult), "G")
                return xns, ("G", cG)
            xns = xn[:, t * D:(t + 1) * D]
            self.w(nc.vector, "A", cA)
            self.wk(nc.vector, ("xnrd", t))
            nc.vector.drain()
            cV = self.inc(nc.vector.tensor_scalar(xns, xa, st[:, 2:3], st[:, 6:7],
                                                  OP.subtract, OP.mult), "V")
            return xns, cV

        def ln_tile(slot, t, cIn):
            cA = ln_heavy(slot, t, cIn)
            self.w(nc.vector, "A", cA)
            cA = ln_smalls([t])
            return ln_xn(slot, t, cA)

        def transposes_to_xnt(xns, t, cV):
            if isinstance(cV, tuple):
                self.w(nc.tensor, cV[0], cV[1])
            else:
                self.w(nc.tensor, "V", cV)
            self.wk(nc.tensor, "PTrd")
            self.wk(nc.tensor, "xntst")
            ins = None
            for c in range(NCH):
                ins = nc.tensor.transpose(PT[:, c * 128:(c + 1) * 128],
                                          xns[:, c * 128:(c + 1) * 128], idents[:])
            cT = self.inc(ins, "T")
            self.setev(("xnrd", t), "T", cT)
            self.w(nc.scalar, "T", cT)
            nc.scalar.drain()
            dst = bass.AP(xnt, t * 128, [[NCH * TG, 128], [TG, NCH], [1, 128]])
            src = bass.AP(PT, 0, [[1024, 128], [128, NCH], [1, 128]])
            cA = self.inc(nc.scalar.copy(dst, src), "A")
            self.setev("PTrd", "A", cA)
            return cA

        qeng = {"in": nc.sync, "wt": nc.gpsimd, "st": nc.scalar}

        def load_tile(slot, src2d, q="in", extra=None, half=None):
            eng = qeng[q]
            self.qfence(q)
            if extra is not None:
                self.w(eng, extra[0], extra[1])
            pe = self.slot_evt.get(slot)
            if pe is not None:
                self.w(eng, pe[0], pe[1])
                self.w(eng, pe[2], pe[3])
            dst = (xbuf[:, slot * D:(slot + 1) * D] if half is None else
                   xbuf[:, slot * D:slot * D + 512])
            return (q, self.inc(eng.dma_start(out=dst, in_=src2d), q, 16))

        def loadln_f(src_ap, tib, g, t, q="in", slot=None, extra=None):
            i = tib[0] + g * NTIL + t
            if slot is None:
                slot = i % 5
            cIn = load_tile(slot, src_ap[bi, tok0(g, t):tok0(g, t) + 128, :], q, extra)
            xns, cV = ln_tile(slot, t, cIn)
            return slot, xns, cV, cIn

        rb1 = bass.AP(rbs, 0, [[16, 128], [1, NH]])
        rbanks = [PL[:, 64:80], PA[:, 0:16], PA[:, 512:528], PL[:, 256:272]]
        cVadd = [0, 0, 0, 0]

        def route_tile(t, cAxt):
            self.w(nc.tensor, "A", cAxt)
            self.wk(nc.tensor, ("PLlog", t))
            if t == 3:
                self.w(nc.tensor, "V", cVadd[0])
            lg = rbanks[t]
            ins = None
            for k in range(NCH):
                ins = nc.tensor.matmul(
                    lg, xnt[:, k * TG + t * 128:k * TG + (t + 1) * 128],
                    rwts[:, k * NH:(k + 1) * NH],
                    start=(k == 0), stop=(k == NCH - 1))
            cT = self.inc(ins, "T")
            self.w(nc.vector, "T", cT)
            nc.vector.drain()
            cVadd[t] = self.inc(nc.vector.tensor_tensor(
                penb[:, t * 32:t * 32 + 16], lg, rb1, OP.add), "V")
            nc.vector.drain()
            nc.vector.max(out=max8[:], in_=penb[:, t * 32:t * 32 + 16])
            nc.vector.drain()
            nc.vector.tensor_scalar(penb[:, t * 32:t * 32 + 16],
                                    penb[:, t * 32:t * 32 + 16],
                                    max8[:, 3:4], BIG, OP.is_ge, OP.mult)

        def route_fin():
            pen4 = bass.AP(penb, 0, [[128, 128], [32, NTIL], [1, NH]])
            nc.vector.drain()
            nc.vector.tensor_scalar(pen4, pen4, BIG, None, OP.subtract)
            self.wk(nc.vector, "penb16rd")
            nc.vector.drain()
            cVp = self.inc(nc.vector.tensor_copy(penb16[:], penb[:]), "V")
            self.setev(("PLlog", 0), "V", cVadd[3])
            self.setev(("PLlog", 3), "V", cVadd[3])
            self.setev(("PLlog", 1), "V", cVadd[1])
            self.setev(("PLlog", 2), "V", cVadd[2])
            self.setev(("PAh", 0), "V", cVadd[1])
            self.setev(("PAh", 1), "V", cVadd[2])
            self.setev(("pyev", 0), "V", cVp)
            self.setev(("pyev", 1), "V", cVp)
            self.w(nc.tensor, "V", cVp)
            self.wk(nc.tensor, "PTpen")
            self.wk(nc.tensor, "PTrd")
            cT = self.inc(nc.tensor.transpose(PT[0:128, 0:128],
                                              penb16[:], idents[:]), "T")
            self.setev("penb16rd", "T", cT)
            self.w(nc.scalar, "T", cT)
            cA_pent = 0
            for t in range(NTIL):
                nc.scalar.drain()
                cA_pent = self.inc(nc.scalar.copy(
                    a_pent[:, t * 128:(t + 1) * 128],
                    PT[t * 32:t * 32 + 16, 0:128]), "A")
            self.setev("PTpen", "A", cA_pent)
            self.setev("PTrd", "A", cA_pent)
            return cA_pent

        def prep_route(cAx):
            for t in range(NTIL):
                route_tile(t, cAx[t])
            return route_fin()

        def phase_F(src_ap, dst_ap, tib, info=None, cAx=None, cA_pent=None, cW=None,
                    tail_load=None):
            if cW is None:
                cW = self.inc(gld(out=a_b2, in_=db2.unsqueeze(0)), "wt", 16)

            def loadln(g, t):
                return loadln_f(src_ap, tib, g, t)

            if info is None:
                info = []
                for t in range(NTIL):
                    i = tib[0] + t
                    slot = i % 5
                    qa, qb = (("in", "wt"), ("st", "in"),
                              ("wt", "st"), ("in", "wt"))[t]
                    src = src_ap[bi, tok0(0, t):tok0(0, t) + 128, :]
                    ca = load_tile(slot, src[:, 0:512], qa, half=0)
                    self.qfence(qb)
                    cb = self.inc(qeng[qb].dma_start(
                        out=xbuf[:, slot * D + 512:(slot + 1) * D],
                        in_=src[:, 512:1024]), qb, 16)
                    cpair = (ca, (qb, cb))
                    xns, cV = ln_tile(slot, t, cpair)
                    info.append((slot, xns, cV, cpair))
            slots = [info[t][0] for t in range(NTIL)]
            if cAx is None:
                cAx = [transposes_to_xnt(info[t][1], t, info[t][2])
                       for t in range(NTIL)]
            if cA_pent is None:
                cA_pent = prep_route(cAx)
            for g in range(NGRP):
                cA_h = []
                self.w(nc.tensor, "wt", cW)
                for c in range(HCH):
                    ph = PA[:, (c % 2) * 512:(c % 2 + 1) * 512]
                    if len(cA_h) >= 2:
                        self.w(nc.tensor, "A", cA_h[-2])
                    else:
                        self.wk(nc.tensor, ("PAh", c % 2))
                    for k in range(NCH):
                        nc.tensor.matmul(ph,
                                         w1s[:, k * HF + c * 128:k * HF + (c + 1) * 128],
                                         xnt[:, k * TG:(k + 1) * TG],
                                         start=(k == 0), stop=False)
                    if c == 0:
                        self.w(nc.tensor, "A", cA_pent)
                    ins = nc.tensor.matmul(ph, a_ind[:, (c % 8) * 128:(c % 8 + 1) * 128],
                                           a_pent[:, 0:TG],
                                           start=False, stop=True)
                    cT = self.inc(ins, "T")
                    self.w(nc.scalar, "T", cT)
                    nc.scalar.drain()
                    cA = self.inc(nc.scalar.activation(a_hT[:, c * TG:(c + 1) * TG],
                                                       ph, AF.Relu,
                                                       bias=b1s[:, c:c + 1]), "A")
                    cA_h.append(cA)
                    self.setev(("PAh", c % 2), "A", cA)
                nxt = []
                cAx2 = []
                for t in range(NTIL):
                    slot = slots[t]
                    py = PB if t % 2 == 0 else PC
                    self.w(nc.tensor, "A", cA_h[-1])
                    self.wk(nc.tensor, ("pyev", t % 2))
                    for half in range(2):
                        nc.tensor.matmul(py[:, half * 512:(half + 1) * 512], ones1[:],
                                         a_b2[0:1, half * 512:(half + 1) * 512],
                                         start=True, stop=False)
                    ins = None
                    for c in range(HCH):
                        for half in range(2):
                            ins = nc.tensor.matmul(
                                py[:, half * 512:(half + 1) * 512],
                                a_hT[:, c * TG + t * 128:c * TG + (t + 1) * 128],
                                w2s[:, c * D + half * 512:c * D + (half + 1) * 512],
                                start=False, stop=(c == HCH - 1))
                    cT = self.inc(ins, "T")
                    xa = xbuf[:, slot * D:(slot + 1) * D]
                    self.w(nc.vector, "T", cT)
                    wpairs(nc.vector, info[t][3])
                    nc.vector.drain()
                    cG = self.inc(nc.vector.scalar_tensor_tensor(
                        xa, py[:, 0:1024], 0.5, xa, OP.mult, OP.add), "V")
                    self.setev(("pyev", t % 2), "V", cG)
                    self.w(nc.scalar, "V", cG)
                    self.qfence("st")
                    cS = self.inc(nc.scalar.dma_start(
                        out=dst_ap[bi, tok0(g, t):tok0(g, t) + 128, :], in_=xa),
                        "st", 16)
                    self.slot_evt[slots[t]] = ("V", cG, "st", cS)
                    if g + 1 < NGRP:
                        for tt in ([0, 1] if t == 0 else
                                   [t + 1] if t + 1 < NTIL else []):
                            nxt.append(loadln(g + 1, tt))
                        while len(cAx2) < len(nxt) - 2:
                            tt = len(cAx2)
                            e = nxt[tt]
                            cAx2.append(transposes_to_xnt(e[1], tt, e[2]))
                            if len(cAx2) >= 2:
                                route_tile(len(cAx2) - 2, cAx2[-2])
                    elif tail_load is not None:
                        tail_load(t)
                if g + 1 < NGRP:
                    while len(cAx2) < NTIL:
                        tt = len(cAx2)
                        e = nxt[tt]
                        cAx2.append(transposes_to_xnt(e[1], tt, e[2]))
                        if len(cAx2) >= 2:
                            route_tile(len(cAx2) - 2, cAx2[-2])
                    route_tile(NTIL - 1, cAx2[NTIL - 1])
                    slots = [e[0] for e in nxt]
                    info = nxt
                    cA_pent = route_fin()
            tib[0] += NGRP * NTIL

        ftmp = xbuf[:, 4 * D:4 * D + 512]
        ftm2 = xbuf[:, 4 * D + 512:5 * D]

        def rope(dst_rot, pp, g, t, cT_proj):
            par = t % 2
            hv = lambda ap, fs, off: bass.AP(ap.tensor, ap.offset + off,
                                             [[fs, 128], [64, NH], [1, 32]])
            pq1 = bass.AP(pp, 0, [[1024, 128], [64, NH], [1, 32]])
            pq2 = bass.AP(pp, 32, [[1024, 128], [64, NH], [1, 32]])
            f1 = bass.AP(ftmp.tensor, ftmp.offset, [[5 * D, 128], [32, NH], [1, 32]])
            f2 = bass.AP(ftm2.tensor, ftm2.offset, [[5 * D, 128], [32, NH], [1, 32]])
            r1, r2 = hv(dst_rot, 2048, 0), hv(dst_rot, 2048, 32)
            cosap = bass.AP(coss, (g * NTIL + t) * 32, [[512, 128], [0, NH], [1, 32]])
            sinap = bass.AP(sins, (g * NTIL + t) * 32, [[512, 128], [0, NH], [1, 32]])
            self.w(nc.vector, "T", cT_proj)
            nc.vector.drain()
            nc.vector.tensor_tensor(f1, pq1, cosap, OP.mult)
            nc.vector.drain()
            nc.vector.tensor_tensor(f2, pq2, sinap, OP.mult)
            nc.vector.drain()
            nc.vector.tensor_sub(f1, f1, f2)
            self.wk(nc.vector, ("abufT", par))
            self.wk(nc.vector, ("abufst", par))
            nc.vector.drain()
            nc.vector.tensor_scalar_max(r1, f1, 0.0)
            nc.vector.drain()
            nc.vector.tensor_tensor(f1, pq1, sinap, OP.mult)
            nc.vector.drain()
            cVpp = self.inc(nc.vector.tensor_tensor(f2, pq2, cosap, OP.mult), "V")
            self.setev(("pp", 0 if pp is PA else 1), "V", cVpp)
            nc.vector.drain()
            nc.vector.tensor_add(f1, f1, f2)
            nc.vector.drain()
            cVr = self.inc(nc.vector.tensor_scalar_max(r2, f1, 0.0), "V")
            return cVr

        def loadln_q(g, t, q="in", slot=None):
            i = tib[0] + g * NTIL + t
            if slot is None:
                slot = i % NSLOT
            cIn = load_tile(slot, dxr[bi, tok0(g, t):tok0(g, t) + 128, :], q)
            xns, cV = ln_tile(slot, t, cIn)
            self.slot_evt[slot] = ("V", cV, "V", cV)
            return slot, xns, cV

        tib = [0]
        infoq = []
        phase_F(dx, dxr, tib,
                tail_load=lambda t: infoq.append(
                    loadln_q(0, t, q=("in", "wt", "in", "st")[t],
                             slot=(2, 3, 1, 0)[t])))

        cW = 0

        def a_proj(t, cin_or_cQx, cax=0):
            pp = PA if t % 2 == 0 else PB
            self.w(nc.tensor, "wt", cW)
            if cin_or_cQx:
                self.w(nc.tensor, "in", cin_or_cQx)
            if cax:
                self.w(nc.tensor, "A", cax)
            self.wk(nc.tensor, ("pp", t % 2))
            ins = None
            for k in range(NCH):
                for half in range(2):
                    ins = nc.tensor.matmul(
                        pp[:, half * 512:(half + 1) * 512],
                        xnt[:, k * TG + t * 128:k * TG + (t + 1) * 128],
                        a_wcur[:, k * D + half * 512:k * D + (half + 1) * 512],
                        start=(k == 0), stop=(k == NCH - 1))
            return pp, self.inc(ins, "T")

        def qtt_flush(pend):
            g, t, cVr = pend
            par = t % 2
            slot_r = abuf[:, par * 1024:(par + 1) * 1024]
            self.w(nc.tensor, "V", cVr)
            self.wk(nc.tensor, "PTrd")
            ins = None
            for c in range(NCH):
                ins = nc.tensor.transpose(PT[:, c * 128:(c + 1) * 128],
                                          slot_r[:, c * 128:(c + 1) * 128],
                                          idents[:])
            cT = self.inc(ins, "T")
            self.setev(("abufT", par), "T", cT)
            self.w(nc.scalar, "T", cT)
            nc.scalar.drain()
            cA = self.inc(nc.scalar.copy(slot_r, PT[:, 0:1024]), "A")
            self.setev("PTrd", "A", cA)
            self.w(nc.scalar, "A", cA)
            self.qfence("st")
            cS = self.inc(nc.scalar.dma_start(
                out=bass.AP(dqt.tensor, g * 128 * NCH * TG + t * 128,
                            [[NCH * TG, 128], [TG, NCH], [1, 128]]),
                in_=slot_r.rearrange("p (c f) -> p c f", c=NCH)), "st", 16)
            self.setev(("abufst", par), "st", cS)


        self.w(nc.gpsimd, "T", self.cnt["T"])
        cW = self.inc(gld(out=a_wcur, in_=chunk_ap(dwq, NCH, D)), "wt", 16)
        cAxq = [transposes_to_xnt(infoq[t][1], t, infoq[t][2]) for t in range(NTIL)]
        self.barrier()
        pendq = []
        for g in range(NGRP):
            self.qfence("in")
            self.w(nc.sync, "A", cAxq[NTIL - 1])
            cst_g = self.inc(ld(out=dxnt[g], in_=xnt[:]), "in", 16)
            cst = [cst_g] * NTIL
            nxt = []
            cAx_next = []
            for t in range(NTIL):
                pp, cT = a_proj(t, 0, cax=cAxq[t])
                if len(pendq) >= 2:
                    qtt_flush(pendq.pop(0))
                cVr = rope(abuf[:, (t % 2) * 1024:(t % 2 + 1) * 1024], pp, g, t, cT)
                pendq.append((g, t, cVr))
                if g + 1 < NGRP:
                    nxt.append(loadln_q(g + 1, t))
                    if t >= 1:
                        e = nxt[t - 1]
                        self.w(nc.tensor, "in", cst[t - 1])
                        cAx_next.append(transposes_to_xnt(e[1], t - 1, e[2]))
            if g + 1 < NGRP:
                e = nxt[NTIL - 1]
                self.w(nc.tensor, "in", cst[NTIL - 1])
                cAx_next.append(transposes_to_xnt(e[1], NTIL - 1, e[2]))
                cAxq = cAx_next
                infoq = nxt
        while pendq:
            qtt_flush(pendq.pop(0))
        tib[0] += NGRP * NTIL
        self.w(nc.gpsimd, "T", self.cnt["T"])
        cW_next = self.inc(gld(out=a_wcur, in_=chunk_ap(dwk, NCH, D)), "wt", 16)
        self.barrier()
        self.ev.pop(("pp", 0), None)
        self.ev.pop(("pp", 1), None)

        cW = cW_next
        def xnt_tile_ap(g, tt):
            return (bass.AP(xnt, tt * 128, [[NCH * TG, 128], [TG, NCH], [1, 128]]),
                    bass.AP(dxnt.tensor, g * 128 * NCH * TG + tt * 128,
                            [[NCH * TG, 128], [TG, NCH], [1, 128]]))

        def ksum_flush(gp, tp, cVrp):
            par = tp % 2
            slot_rp = abuf[:, par * 1024:(par + 1) * 1024]
            self.w(nc.tensor, "V", cVrp)
            ti = gp * NTIL + tp
            ins = None
            for half in range(2):
                ins = nc.tensor.matmul(
                    PC[64:65, half * 512:(half + 1) * 512], onesc[:],
                    slot_rp[:, half * 512:(half + 1) * 512],
                    start=(ti == 0), stop=(ti == NGRP * NTIL - 1))
            cTk = self.inc(ins, "T")
            self.setev(("abufT", par), "T", cTk)
            self.w(nc.scalar, "V", cVrp)
            self.qfence("st")
            cS = self.inc(nc.scalar.dma_start(
                out=dkr[tok0(gp, tp):tok0(gp, tp) + 128, :], in_=slot_rp), "st", 16)
            self.setev(("abufst", par), "st", cS)

        ksq = []
        cT_projk = [0, 0, 0, 0]
        for g in range(NGRP):
            self.qfence("in")
            self.w(nc.sync, "T")
            cQx_g = self.inc(ld(out=xnt[:], in_=dxnt[g]), "in", 16)
            for t in range(NTIL):
                pp, cT = a_proj(t, cQx_g)
                if g == NGRP - 1:
                    cT_projk[t] = cT
                if len(ksq) >= 2:
                    ksum_flush(*ksq.pop(0))
                cVr = rope(abuf[:, (t % 2) * 1024:(t % 2 + 1) * 1024], pp, g, t, cT)
                cT_lastproj = cT
                ksq.append((g, t, cVr))
        while ksq:
            ksum_flush(*ksq.pop(0))
        self.w(nc.gpsimd, "T", cT_lastproj)
        cW_next = self.inc(gld(out=a_wcur, in_=chunk_ap(dwv, NCH, D)), "wt", 16)
        self.qfence("in")
        self.w(nc.sync, "T", cT_projk[NTIL - 1])
        cQxs0 = [("in", self.inc(ld(out=xnt[:], in_=dxnt[0]), "in", 16))] * NTIL
        self.qfence("in")
        cKpre = self.inc(ld(out=xn[:, 2 * D:3 * D], in_=dkr[0:128, :]), "in", 16)
        self.barrier()
        self.ev.pop(("pp", 0), None)
        self.ev.pop(("pp", 1), None)

        cW = cW_next
        last = NGRP * NTIL - 1

        def kv_mm(g, t, cA_v, cK):
            vs = xn[:, (t % 2) * D:(t % 2 + 1) * D]
            ks = xn[:, (2 + t % 2) * D:(3 + t % 2) * D]
            self.w(nc.tensor, "A", cA_v)
            self.w(nc.tensor, "in", cK)
            ti = g * NTIL + t
            ins = None
            for h in range(NH):
                ins = nc.tensor.matmul(PC[0:64, h * 64:(h + 1) * 64],
                                       vs[:, h * 64:(h + 1) * 64],
                                       ks[:, h * 64:(h + 1) * 64],
                                       start=(ti == 0 and h % 8 == 0),
                                       stop=(ti == last and h % 8 == 7))
            cT = self.inc(ins, "T")
            self.setev(("kvv", t % 2), "T", cT)
            self.setev(("kvk", t % 2), "T", cT)

        pendkv = None
        cKnext = [0, 0]
        for g in range(NGRP):
            if g == 0:
                cQxs = cQxs0
            else:
                self.qfence("in")
                self.w(nc.sync, "T")
                cQxs = [("in", self.inc(ld(out=xnt[:], in_=dxnt[g]), "in", 16))] * NTIL
            for t in range(NTIL):
                ti_g = g * NTIL + t
                cK = cKpre if ti_g == 0 else cKnext[ti_g % 2]
                pp = PA if t % 2 == 0 else PB
                self.w(nc.tensor, "wt", cW)
                self.w(nc.tensor, cQxs[t][0], cQxs[t][1])
                self.wk(nc.tensor, ("pp", t % 2))
                ins = None
                for k in range(NCH):
                    for half in range(2):
                        ins = nc.tensor.matmul(
                            pp[:, half * 512:(half + 1) * 512],
                            xnt[:, k * TG + t * 128:k * TG + (t + 1) * 128],
                            a_wcur[:, k * D + half * 512:k * D + (half + 1) * 512],
                            start=(k == 0), stop=(k == NCH - 1))
                cT = self.inc(ins, "T")
                vs = xn[:, (t % 2) * D:(t % 2 + 1) * D]
                self.w(nc.scalar, "T", cT)
                self.wk(nc.scalar, ("kvv", t % 2))
                nc.scalar.drain()
                cA = self.inc(nc.scalar.activation(vs, pp[:, 0:1024], AF.Copy), "A")
                self.setev(("pp", t % 2), "A", cA)
                if pendkv is not None:
                    kv_mm(*pendkv)
                if ti_g + 1 <= last:
                    tn = ti_g + 1
                    gn, ttn = tn // NTIL, tn % NTIL
                    ksn = xn[:, (2 + tn % 2) * D:(3 + tn % 2) * D]
                    self.qfence("in")
                    self.wk(nc.sync, ("kvk", tn % 2))
                    cKnext[tn % 2] = self.inc(
                        ld(out=ksn, in_=dkr[tok0(gn, ttn):tok0(gn, ttn) + 128, :]),
                        "in", 16)
                pendkv = (g, t, cA, cK)
        self.w(nc.gpsimd, "T", self.cnt["T"])
        cW_wo = self.inc(gld(out=a_wo, in_=chunk_ap(dwo, NCH, D)), "wt", 16)
        kv_mm(*pendkv)
        self.w(nc.scalar, "T")
        nc.scalar.drain()
        cA_kv = self.inc(nc.scalar.activation(abuf[0:65, 0:1024], PC[0:65, 0:1024],
                                              AF.Copy), "A")
        nc.scalar.drain()
        cA_kv = self.inc(nc.scalar.activation(abuf[64:128, 1024:2048],
                                              PC[0:64, 0:1024], AF.Copy), "A")
        self.w(nc.vector, "A", cA_kv)
        nc.vector.drain()
        cV = self.inc(nc.vector.memset(bds[:], 0.0), "V")
        self.w(nc.tensor, "A", cA_kv)
        self.w(nc.tensor, "V", cV)
        ins = None
        for c in range(NCH):
            ins = nc.tensor.matmul(PL[:, c:c + 1],
                                   abuf[64:65, c * 128:(c + 1) * 128],
                                   onesc[64:65, 0:1], start=True, stop=True)
        cT = self.inc(ins, "T")
        self.w(nc.scalar, "T", cT)
        cA_bd = 0
        for c in range(NCH):
            for hh in range(2):
                h = 2 * c + hh
                nc.scalar.drain()
                cA_bd = self.inc(nc.scalar.copy(
                    bds[hh * 64:(hh + 1) * 64, c * NH + h:c * NH + h + 1],
                    PL[hh * 64:(hh + 1) * 64, c:c + 1]), "A")
        self.barrier()

        cA_m = 0
        for c in range(NCH):
            pm = PA if c % 2 == 0 else PC
            self.w(nc.tensor, "wt", cW_wo)
            if c == 1:
                self.w(nc.tensor, "A", cA_kv)
            self.wk(nc.tensor, ("PAm", c % 2))
            ins = None
            for hh in range(2):
                h = 2 * c + hh
                kvsl = (abuf[0:64, h * 64:(h + 1) * 64] if h % 2 == 0 else
                        abuf[64:128, 1024 + h * 64:1024 + (h + 1) * 64])
                for half in range(2):
                    ins = nc.tensor.matmul(
                        pm[hh * 64:(hh + 1) * 64, half * 512:(half + 1) * 512],
                        kvsl,
                        a_wo[(h % 2) * 64:(h % 2) * 64 + 64,
                             (h // 2) * 1024 + half * 512:(h // 2) * 1024 + (half + 1) * 512],
                        start=True, stop=True)
            cT = self.inc(ins, "T")
            self.setev("kvread", "T", cT)
            self.w(nc.scalar, "T", cT)
            nc.scalar.drain()
            cA_m = self.inc(nc.scalar.copy(a_m[:, c * 1024:(c + 1) * 1024],
                                           pm[:, 0:1024]), "A")
            self.setev(("PAm", c % 2), "A", cA_m)
        def b_loads(lc):
            self.qfence("in")
            self.wk(nc.sync, "qsread")
            cQ = self.inc(ld(out=xnt[:], in_=dqt[lc]), "in", 16)
            return [("in", cQ)]

        def b_den(lc, cQ):
            for qq, c in cQ:
                self.w(nc.tensor, qq, c)
            self.w(nc.tensor, "A", cA_bd)
            self.wk(nc.tensor, "PLden")
            ins = None
            for c in range(NCH):
                ins = nc.tensor.matmul(PL[0:16, 0:512],
                                       bds[:, c * NH:(c + 1) * NH],
                                       xnt[:, c * TG:(c + 1) * TG],
                                       start=(c == 0), stop=(c == NCH - 1))
            cT = self.inc(ins, "T")
            self.w(nc.vector, "T", cT)
            self.w(nc.vector, "A", cA_m)
            nc.vector.drain()
            nc.vector.tensor_scalar(PL[0:16, 0:512], PL[0:16, 0:512], EPS_AT,
                                    None, OP.add)
            nc.vector.drain()
            with nc.allow_low_precision(reason="recip_den bf16 matmul operand"):
                cV_r = self.inc(nc.vector.reciprocal(a_rd[:, lc * TG:(lc + 1) * TG],
                                                     PL[0:16, 0:512]), "V")
            self.setev("PLden", "V", cV_r)
            return cV_r

        def b_bc_qscale(lc, cV_r):
            cV_qs = 0
            for c in range(NCH):
                pbh = PB[:, (c % 2) * 512:(c % 2 + 1) * 512]
                self.wk(nc.tensor, ("pbh", c % 2))
                self.w(nc.tensor, "V", cV_r)
                cT = self.inc(nc.tensor.matmul(pbh, a_ind[:, c * 128:(c + 1) * 128],
                                               a_rd[:, lc * TG:(lc + 1) * TG],
                                               start=True, stop=True), "T")
                self.w(nc.vector, "T", cT)
                self.wk(nc.vector, ("qsrd", c))
                nc.vector.drain()
                cV_qs = self.inc(nc.vector.tensor_tensor(
                    xn[:, c * TG:(c + 1) * TG], xnt[:, c * TG:(c + 1) * TG],
                    pbh, OP.mult), "V")
                self.setev(("pbh", c % 2), "V", cV_qs)
            self.setev("qsread", "V", cV_qs)
            return cV_qs

        cQn = b_loads(0)
        cV_rn = b_den(0, cQn)
        cS_b0 = []
        infoF2t = []

        def b_tail_load(t):
            slot = (3, 4, 0, 1)[t]
            cIn = load_tile(slot, dxr[bi, tok0(0, t):tok0(0, t) + 128, :],
                            ("wt", "in", "wt", "st")[t], extra=("st", cS_b0[t]))
            self.wk(nc.scalar, "kvread")
            cA1 = ln_heavy(slot, t, cIn,
                           sqout=abuf[:, (t % 2) * 1024:(t % 2 + 1) * 1024])
            self.w(nc.vector, "A", cA1)
            cA2 = ln_smalls([t])
            infoF2t.append((slot, cA2, cIn))

        for lc in range(NGRP):
            cV_qs = b_bc_qscale(lc, cV_rn)
            for t in range(NTIL):
                i = tib[0] + lc * NTIL + t
                cIn = load_tile(i % 5, dxr[bi, tok0(lc, t):tok0(lc, t) + 128, :])
                pao = PA if t % 2 == 0 else PC
                self.w(nc.tensor, "V", cV_qs)
                self.w(nc.tensor, "A", cA_m)
                self.wk(nc.tensor, ("paoev", t % 2))
                ins = None
                for c in range(NCH):
                    for half in range(2):
                        ins = nc.tensor.matmul(
                            pao[:, half * 512:(half + 1) * 512],
                            xn[:, c * TG + t * 128:c * TG + (t + 1) * 128],
                            a_m[:, c * 1024 + half * 512:c * 1024 + (half + 1) * 512],
                            start=(c == 0), stop=(c == NCH - 1))
                cT = self.inc(ins, "T")
                if t == NTIL - 1:
                    for c in range(NCH):
                        self.setev(("qsrd", c), "T", cT)
                xa = xbuf[:, (i % 5) * D:((i % 5) + 1) * D]
                self.w(nc.vector, "T", cT)
                self.w(nc.vector, cIn[0], cIn[1])
                nc.vector.drain()
                cG = self.inc(nc.vector.tensor_add(xa, xa, pao[:, 0:1024]), "V")
                self.setev(("paoev", t % 2), "V", cG)
                self.w(nc.scalar, "V", cG)
                self.qfence("st")
                cS = self.inc(nc.scalar.dma_start(
                    out=dxr[bi, tok0(lc, t):tok0(lc, t) + 128, :], in_=xa), "st", 16)
                if lc == 0:
                    cS_b0.append(cS)
                self.slot_evt[i % 5] = ("V", cG, "st", cS)
                if lc + 1 < NGRP:
                    if t == 0:
                        cQn = b_loads(lc + 1)
                    elif t == 1:
                        cV_rn = b_den(lc + 1, cQn)
                else:
                    cT_attn_last = cT
                    b_tail_load(t)
        tib[0] += NGRP * NTIL
        for t in range(NTIL):
            self.ev[("xnrd", t)] = ("T", cT_attn_last)
        infoF2 = []
        cAxF2 = []
        for t in range(NTIL):
            slot, cA2, cIn = infoF2t[t]
            xns, cV = ln_xn(slot, t, cA2)
            infoF2.append((slot, xns, cV, cIn))
            cAxF2.append(transposes_to_xnt(xns, t, cV))
        self.ev[("PLlog", 0)] = self.ev["PLden"]
        self.ev[("PLlog", 1)] = self.ev[("paoev", 0)]
        self.ev[("PLlog", 2)] = self.ev[("paoev", 0)]
        self.ev[("PLlog", 3)] = self.ev["PLden"]
        cA_pentF2 = prep_route(cAxF2)
        self.barrier()

        phase_F(dxr, dout, tib, infoF2, cAxF2, cA_pentF2, cW=0)
        self.barrier()

        for s in self.sem:
            nc.regs_alu(regs[s], regs[s], self.cnt[s], op=OP.add)
        fori_cm.__exit__(None, None, None)
        es.close()
        return nc


def _host_prep(inputs):
    perm = np.zeros(D, np.int64)
    for h in range(NH):
        for j in range(32):
            perm[h * 64 + j] = h * 64 + 2 * j
            perm[h * 64 + 32 + j] = h * 64 + 2 * j + 1
    freqs = 1.0 / 10000.0 ** (np.arange(0, 64, 2, dtype=np.float64) / 64)
    ang = np.arange(S, dtype=np.float64)[:, None] * freqs[None, :]
    cos64 = np.cos(ang).reshape(16, 128, 32)
    sin64 = np.sin(ang).reshape(16, 128, 32)
    indc = np.zeros((NH, NCH * 128), np.float32)
    for c in range(NCH):
        indc[2 * c, c * 128:c * 128 + 64] = 1.0
        indc[2 * c + 1, c * 128 + 64:(c + 1) * 128] = 1.0
    psi = np.zeros(HF, np.int64)
    for c in range(HCH):
        el = 2 * (c % 8)
        psi[c * 128:c * 128 + 64] = el * 256 + (c // 8) * 64 + np.arange(64)
        psi[c * 128 + 64:(c + 1) * 128] = (el + 1) * 256 + (c // 8) * 64 + np.arange(64)
    return {
        "w1t": np.ascontiguousarray(np.asarray(inputs["L0_ffn_w1"]).T[:, psi]),
        "w2t": np.ascontiguousarray(np.asarray(inputs["L0_ffn_w2"]).T[psi, :]),
        "wq": np.ascontiguousarray(np.asarray(inputs["L0_attn_Wq"])[:, perm]),
        "wk": np.ascontiguousarray(np.asarray(inputs["L0_attn_Wk"])[:, perm]),
        "wv": np.ascontiguousarray(np.asarray(inputs["L0_attn_Wv"])),
        "wo": np.ascontiguousarray(np.asarray(inputs["L0_attn_Wo"])),
        "rwt": np.ascontiguousarray(np.asarray(inputs["L0_ffn_router_w"]).T),
        "rb": np.ascontiguousarray(np.asarray(inputs["L0_ffn_router_b"])),
        "b1": np.ascontiguousarray(np.asarray(inputs["L0_ffn_b1"])[psi]),
        "b2": np.ascontiguousarray(np.asarray(inputs["L0_ffn_b2"])),
        "ident": np.eye(128, dtype=np.float32).astype(BF),
        "cost": cos64.astype(np.float32),
        "sint": sin64.astype(np.float32),
        "costl": (cos64 - cos64.astype(np.float32).astype(np.float64)).astype(np.float32),
        "sintl": (sin64 - sin64.astype(np.float32).astype(np.float64)).astype(np.float32),
        "indc": indc.astype(BF),
    }


def kernel(**inputs):
    x = np.asarray(inputs["L0_attn_x"])
    common = _host_prep(inputs)
    nc = _B().build()
    in_maps = []
    for i in range(8):
        m = dict(common)
        m["x"] = np.ascontiguousarray(x[i * NB:(i + 1) * NB])
        in_maps.append(m)
    res = run_bass_kernel_spmd(nc, in_maps, core_ids=list(range(8)))
    return np.concatenate([r["out"] for r in res.results], axis=0)


# version 35
# speedup vs baseline: 1.0157x; 1.0157x over previous
import numpy as np
import ml_dtypes
from contextlib import ExitStack
import concourse.bass as bass
import concourse.mybir as mybir
from concourse.bass_utils import run_bass_kernel_spmd

F32 = mybir.dt.float32
BF16 = mybir.dt.bfloat16
AF = mybir.ActivationFunctionType
OP = mybir.AluOpType
AX = mybir.AxisListType
BF = ml_dtypes.bfloat16

D = 1024
HF = 4096
NH = 16
S = 2048
NB = 4
NCH = 8
HCH = 32
TG = 512
NGRP = S // TG
NTIL = TG // 128
NSLOT = 4
EPS_LN = 1e-5
EPS_AT = 1e-6
BIG = 1e9


class _B:
    def __init__(self, nb=NB):
        self.nb = nb
        self.nc = bass.Bass("TRN2", target_bir_lowering=False)
        self.cnt = {}
        self.lastw = {}
        self.base = {}
        self.sem = {}
        self.ev = {}

    def inc(self, inst, s, n=1):
        inst.then_inc(self.sem[s], n)
        self.cnt[s] += n
        return self.cnt[s]

    def w(self, eng, s, c=None):
        if c is None:
            c = self.cnt[s]
        if c <= 0 or c <= self.lastw.get((eng.engine, s), -1):
            return
        if self.base.get(s) is None:
            eng.wait_ge(self.sem[s], c)
        else:
            tmp = self.wtmp[eng.engine]
            eng.reg_alu(tmp, self.regs[s][eng.engine], c, op=OP.add)
            eng.wait_ge(self.sem[s], tmp)
        self.lastw[(eng.engine, s)] = c

    def setev(self, key, sem, cnt):
        self.ev[key] = (sem, cnt)

    def wk(self, eng, key):
        e = self.ev.get(key)
        if e is not None:
            self.w(eng, e[0], e[1])

    def qfence(self, s):
        eng = {"in": self.nc.sync, "wt": self.nc.gpsimd, "st": self.nc.scalar}[s]
        self.w(eng, s)

    def barrier(self):
        nc = self.nc
        self.w(nc.sync, "in")
        self.w(nc.gpsimd, "wt")
        self.w(nc.scalar, "st")
        nc.all_engine_barrier()
        for s in self.sem:
            for eng in (nc.sync, nc.gpsimd, nc.scalar, nc.vector, nc.tensor):
                k = (eng.engine, s)
                self.lastw[k] = max(self.lastw.get(k, -1), self.cnt[s])

    def build(self):
        nc = self.nc
        nb = self.nb
        dx = nc.dram_tensor("x", [nb, S, D], F32, kind="ExternalInput").ap()
        dout = nc.dram_tensor("out", [nb, S, D], F32, kind="ExternalOutput").ap()
        dxr = nc.dram_tensor("xr", [nb, S, D], F32, kind="Internal").ap()
        dqt = nc.dram_tensor("qt", [NGRP, 128, NCH * TG], BF16, kind="Internal").ap()
        dkr = nc.dram_tensor("kr", [S, D], BF16, kind="Internal").ap()
        dxnt = nc.dram_tensor("xnt_s", [NGRP, 128, NCH * TG], BF16, kind="Internal").ap()
        dw1t = nc.dram_tensor("w1t", [D, HF], F32, kind="ExternalInput").ap()
        dw2t = nc.dram_tensor("w2t", [HF, D], F32, kind="ExternalInput").ap()
        dwq = nc.dram_tensor("wq", [D, D], F32, kind="ExternalInput").ap()
        dwk = nc.dram_tensor("wk", [D, D], F32, kind="ExternalInput").ap()
        dwv = nc.dram_tensor("wv", [D, D], F32, kind="ExternalInput").ap()
        dwo = nc.dram_tensor("wo", [D, D], F32, kind="ExternalInput").ap()
        drwt = nc.dram_tensor("rwt", [D, NH], F32, kind="ExternalInput").ap()
        drb = nc.dram_tensor("rb", [NH], F32, kind="ExternalInput").ap()
        db1 = nc.dram_tensor("b1", [HF], F32, kind="ExternalInput").ap()
        db2 = nc.dram_tensor("b2", [D], F32, kind="ExternalInput").ap()
        dident = nc.dram_tensor("ident", [128, 128], BF16, kind="ExternalInput").ap()
        dcos = nc.dram_tensor("cost", [16, 128, 32], F32, kind="ExternalInput").ap()
        dsin = nc.dram_tensor("sint", [16, 128, 32], F32, kind="ExternalInput").ap()
        dcosl = nc.dram_tensor("costl", [16, 128, 32], F32, kind="ExternalInput").ap()
        dsinl = nc.dram_tensor("sintl", [16, 128, 32], F32, kind="ExternalInput").ap()
        dind = nc.dram_tensor("indc", [NH, NCH * 128], BF16, kind="ExternalInput").ap()

        es = ExitStack()
        self.es = es
        sbt = lambda n, shp, dt: es.enter_context(nc.sbuf_tensor(n, shp, dt))
        pst = lambda n, shp, dt: es.enter_context(nc.psum_tensor(n, shp, dt))

        w1s = sbt("w1s", [128, NCH * HF], BF16)
        w2s = sbt("w2s", [128, HCH * D], BF16)
        arena = sbt("arena", [128, 17920], BF16)
        xbuf = sbt("xbuf", [128, 5 * D], F32)
        abuf = sbt("abuf", [128, 2048], BF16)
        xnt = sbt("xnt", [128, NCH * TG], BF16)
        xn = sbt("xn", [128, 4 * D], BF16)
        coss = sbt("coss", [128, 16 * 32], BF16)
        sins = sbt("sins", [128, 16 * 32], BF16)
        rwts = sbt("rwts", [128, NCH * NH], BF16)
        b1s = sbt("b1s", [128, HCH], F32)
        idents = sbt("idents", [128, 128], BF16)
        ones1 = sbt("ones1", [1, 128], BF16)
        onesc = sbt("onesc", [128, 1], BF16)
        bds = sbt("bds", [128, NCH * NH], BF16)
        stats = sbt("stats", [128, 8 * NTIL], F32)
        max8 = sbt("max8", [128, 8], F32)
        penb = sbt("penb", [128, 128], F32)
        penb16 = sbt("penb16", [128, 128], BF16)
        rbs = sbt("rbs", [128, NH], F32)

        a_hT = arena[:, 0:HCH * TG]
        a_b2 = arena[0:1, 16384:16384 + D]
        a_pent = arena[64:80, 17408:17408 + TG]
        a_wcur = arena[:, 0:8192]
        a_wo = arena[:, 0:8192]
        a_m = arena[:, 8192:16384]
        a_rd = arena[64:80, 0:S]
        a_ind = arena[64:80, 16384:16384 + NCH * 128]

        PT = pst("PT", [128, 1024], BF16)
        PL = pst("PL", [128, 512], F32)
        PA = pst("PA", [128, 1024], F32)
        PB = pst("PB", [128, 1024], F32)
        PC = pst("PC", [128, 1024], F32)

        for s in ("in", "wt", "st", "T", "V", "A", "G"):
            self.sem[s] = es.enter_context(nc.semaphore("sem_" + s))
            self.cnt[s] = 0

        with nc.Block() as blk:
            @blk.sync
            def _(sync):
                pass

        ld = nc.sync.dma_start

        def gld(**kw):
            self.qfence("wt")
            return nc.gpsimd.dma_start(**kw)

        def chunk_ap(dap, nk, j, k0=0):
            return bass.AP(dap.tensor, k0 * 128 * j, [[j, 128], [128 * j, nk], [1, j]])

        self.inc(ld(out=idents[:], in_=dident), "in", 16)
        self.inc(ld(out=b1s[:], in_=db1.rearrange("(c p) -> p c", p=128),
                    allow_slow_non_contiguous=True), "in", 16)
        self.inc(ld(out=rbs[:], in_=drb.unsqueeze(0).broadcast_to([128, NH])), "in", 16)
        for h in range(2):
            self.inc(gld(out=w1s[:, h * 4 * HF:(h + 1) * 4 * HF],
                         in_=chunk_ap(dw1t, 4, HF, k0=h * 4)), "wt", 16)
        for h in range(4):
            self.inc(gld(out=w2s[:, h * 8 * D:(h + 1) * 8 * D],
                         in_=chunk_ap(dw2t, 8, D, k0=h * 8)), "wt", 16)
        self.inc(gld(out=rwts[:], in_=chunk_ap(drwt, NCH, NH)), "wt", 16)
        self.inc(gld(out=a_ind, in_=dind), "wt", 16)
        self.inc(gld(out=coss[:], in_=dcos.transpose([1, 0, 2])), "wt", 16)
        self.inc(gld(out=sins[:], in_=dsin.transpose([1, 0, 2])), "wt", 16)
        self.inc(gld(out=coss[:], in_=dcosl.transpose([1, 0, 2]),
                     accum_op=OP.add), "wt", 16)
        self.inc(gld(out=sins[:], in_=dsinl.transpose([1, 0, 2]),
                     accum_op=OP.add), "wt", 16)
        self.inc(nc.vector.memset(penb[:], 0.0), "V")
        nc.vector.drain()
        self.inc(nc.vector.memset(penb16[:], 0.0), "V")
        nc.vector.drain()
        self.inc(nc.vector.memset(ones1[:], 1.0), "V")
        nc.vector.drain()
        self.inc(nc.vector.memset(onesc[:], 1.0), "V")
        self.barrier()

        regs = {s: nc.alloc_registers("base_" + s, engines=mybir.ALL_ENGINES)
                for s in self.sem}
        self.regs = regs
        for s in self.sem:
            nc.regs_mov(regs[s], self.cnt[s])
        self.wtmp = {}
        for eng in (nc.sync, nc.gpsimd, nc.scalar, nc.vector, nc.tensor):
            self.wtmp[eng.engine] = eng.alloc_register("wtmp")

        self.slot_evt = {}
        fori_cm = nc.Fori(0, nb)
        bi = fori_cm.__enter__()
        for s in self.sem:
            self.base[s] = True
            self.cnt[s] = 0
        self.lastw = {}
        self.ev = {}

        def tok0(g, t):
            return g * TG + t * 128

        def wpairs(eng, cin):
            if isinstance(cin[0], tuple):
                for qq, cc in cin:
                    self.w(eng, qq, cc)
            else:
                self.w(eng, cin[0], cin[1])

        def ln_heavy(slot, t, cIn, sqout=None):
            xa = xbuf[:, slot * D:(slot + 1) * D]
            st = stats[:, t * 8:t * 8 + 8]
            wpairs(nc.scalar, cIn)
            if sqout is None:
                self.wk(nc.scalar, ("xnrd", t))
                sqout = xn[:, t * D:(t + 1) * D]
            nc.scalar.drain()
            self.inc(nc.scalar.activation(sqout, xa, AF.Copy,
                                          accum_out=st[:, 0:1]), "A")
            nc.scalar.drain()
            cA = self.inc(nc.scalar.activation(sqout, xa, AF.Square,
                                               accum_out=st[:, 1:2]), "A")
            return cA

        def ln_smalls(ts_):
            n = len(ts_)
            o = ts_[0] * 8
            if n == 1:
                st = stats[:, o:o + 8]
                nc.vector.drain()
                nc.vector.tensor_scalar(st[:, 2:3], st[:, 0:1], 1.0 / D, None, OP.mult)
                nc.vector.drain()
                nc.vector.tensor_scalar(st[:, 3:4], st[:, 2:3], st[:, 2:3], EPS_LN,
                                        OP.mult, OP.subtract)
                nc.vector.drain()
                nc.vector.tensor_scalar(st[:, 4:5], st[:, 1:2], 1.0 / D, st[:, 3:4],
                                        OP.mult, OP.subtract)
                nc.vector.drain()
                cV = self.inc(nc.vector.reciprocal(st[:, 5:6], st[:, 4:5]), "V")
                self.w(nc.scalar, "V", cV)
                nc.scalar.drain()
                return self.inc(nc.scalar.activation(st[:, 6:7], st[:, 5:6],
                                                     AF.Sqrt), "A")
            mr = bass.AP(stats, o + 0, [[8 * NTIL, 128], [8, n], [1, 1]])
            sq = bass.AP(stats, o + 1, [[8 * NTIL, 128], [8, n], [1, 1]])
            m_ = bass.AP(stats, o + 2, [[8 * NTIL, 128], [8, n], [1, 1]])
            msq = bass.AP(stats, o + 3, [[8 * NTIL, 128], [8, n], [1, 1]])
            vpe = bass.AP(stats, o + 4, [[8 * NTIL, 128], [8, n], [1, 1]])
            rv = bass.AP(stats, o + 5, [[8 * NTIL, 128], [8, n], [1, 1]])
            rstd = bass.AP(stats, o + 6, [[8 * NTIL, 128], [8, n], [1, 1]])
            nc.vector.drain()
            nc.vector.tensor_scalar(m_, mr, 1.0 / D, None, OP.mult)
            nc.vector.drain()
            nc.vector.tensor_tensor(msq, m_, m_, OP.mult)
            nc.vector.drain()
            nc.vector.tensor_scalar(msq, msq, EPS_LN, None, OP.subtract)
            nc.vector.drain()
            nc.vector.tensor_scalar(vpe, sq, 1.0 / D, None, OP.mult)
            nc.vector.drain()
            nc.vector.tensor_tensor(vpe, vpe, msq, OP.subtract)
            nc.vector.drain()
            cV = self.inc(nc.vector.reciprocal(rv, vpe), "V")
            self.w(nc.scalar, "V", cV)
            nc.scalar.drain()
            return self.inc(nc.scalar.activation(rstd, rv, AF.Sqrt), "A")

        def ln_xn(slot, t, cA, eng="V"):
            xa = xbuf[:, slot * D:(slot + 1) * D]
            st = stats[:, t * 8:t * 8 + 8]
            xns = xn[:, t * D:(t + 1) * D]
            if eng == "G":
                self.w(nc.gpsimd, "A", cA)
                self.wk(nc.gpsimd, ("xnrd", t))
                nc.gpsimd.drain()
                cG = self.inc(nc.gpsimd.tensor_scalar(
                    xns, xa, st[:, 2:3], st[:, 6:7], OP.subtract, OP.mult), "G")
                return xns, ("G", cG)
            xns = xn[:, t * D:(t + 1) * D]
            self.w(nc.vector, "A", cA)
            self.wk(nc.vector, ("xnrd", t))
            nc.vector.drain()
            cV = self.inc(nc.vector.tensor_scalar(xns, xa, st[:, 2:3], st[:, 6:7],
                                                  OP.subtract, OP.mult), "V")
            return xns, cV

        def ln_tile(slot, t, cIn):
            cA = ln_heavy(slot, t, cIn)
            self.w(nc.vector, "A", cA)
            cA = ln_smalls([t])
            return ln_xn(slot, t, cA)

        def transposes_to_xnt(xns, t, cV):
            if isinstance(cV, tuple):
                self.w(nc.tensor, cV[0], cV[1])
            else:
                self.w(nc.tensor, "V", cV)
            self.wk(nc.tensor, "PTrd")
            self.wk(nc.tensor, "xntst")
            ins = None
            for c in range(NCH):
                ins = nc.tensor.transpose(PT[:, c * 128:(c + 1) * 128],
                                          xns[:, c * 128:(c + 1) * 128], idents[:])
            cT = self.inc(ins, "T")
            self.setev(("xnrd", t), "T", cT)
            self.w(nc.scalar, "T", cT)
            nc.scalar.drain()
            dst = bass.AP(xnt, t * 128, [[NCH * TG, 128], [TG, NCH], [1, 128]])
            src = bass.AP(PT, 0, [[1024, 128], [128, NCH], [1, 128]])
            cA = self.inc(nc.scalar.copy(dst, src), "A")
            self.setev("PTrd", "A", cA)
            return cA

        qeng = {"in": nc.sync, "wt": nc.gpsimd, "st": nc.scalar}

        def load_tile(slot, src2d, q="in", extra=None, half=None):
            eng = qeng[q]
            self.qfence(q)
            if extra is not None:
                self.w(eng, extra[0], extra[1])
            pe = self.slot_evt.get(slot)
            if pe is not None:
                self.w(eng, pe[0], pe[1])
                self.w(eng, pe[2], pe[3])
            dst = (xbuf[:, slot * D:(slot + 1) * D] if half is None else
                   xbuf[:, slot * D:slot * D + 512])
            return (q, self.inc(eng.dma_start(out=dst, in_=src2d), q, 16))

        def loadln_f(src_ap, tib, g, t, q="in", slot=None, extra=None):
            i = tib[0] + g * NTIL + t
            if slot is None:
                slot = i % 5
            cIn = load_tile(slot, src_ap[bi, tok0(g, t):tok0(g, t) + 128, :], q, extra)
            xns, cV = ln_tile(slot, t, cIn)
            return slot, xns, cV, cIn

        rb1 = bass.AP(rbs, 0, [[16, 128], [1, NH]])
        rbanks = [PL[:, 64:80], PA[:, 0:16], PA[:, 512:528], PL[:, 256:272]]
        cVadd = [0, 0, 0, 0]

        def route_tile(t, cAxt):
            self.w(nc.tensor, "A", cAxt)
            self.wk(nc.tensor, ("PLlog", t))
            if t == 3:
                self.w(nc.tensor, "V", cVadd[0])
            lg = rbanks[t]
            ins = None
            for k in range(NCH):
                ins = nc.tensor.matmul(
                    lg, xnt[:, k * TG + t * 128:k * TG + (t + 1) * 128],
                    rwts[:, k * NH:(k + 1) * NH],
                    start=(k == 0), stop=(k == NCH - 1))
            cT = self.inc(ins, "T")
            self.w(nc.vector, "T", cT)
            nc.vector.drain()
            cVadd[t] = self.inc(nc.vector.tensor_tensor(
                penb[:, t * 32:t * 32 + 16], lg, rb1, OP.add), "V")
            nc.vector.drain()
            nc.vector.max(out=max8[:], in_=penb[:, t * 32:t * 32 + 16])
            nc.vector.drain()
            nc.vector.tensor_scalar(penb[:, t * 32:t * 32 + 16],
                                    penb[:, t * 32:t * 32 + 16],
                                    max8[:, 3:4], BIG, OP.is_ge, OP.mult)

        def route_fin():
            pen4 = bass.AP(penb, 0, [[128, 128], [32, NTIL], [1, NH]])
            nc.vector.drain()
            nc.vector.tensor_scalar(pen4, pen4, BIG, None, OP.subtract)
            self.wk(nc.vector, "penb16rd")
            nc.vector.drain()
            cVp = self.inc(nc.vector.tensor_copy(penb16[:], penb[:]), "V")
            self.setev(("PLlog", 0), "V", cVadd[3])
            self.setev(("PLlog", 3), "V", cVadd[3])
            self.setev(("PLlog", 1), "V", cVadd[1])
            self.setev(("PLlog", 2), "V", cVadd[2])
            self.setev(("PAh", 0), "V", cVadd[1])
            self.setev(("PAh", 1), "V", cVadd[2])
            self.setev(("pyev", 0), "V", cVp)
            self.setev(("pyev", 1), "V", cVp)
            self.w(nc.tensor, "V", cVp)
            self.wk(nc.tensor, "PTpen")
            self.wk(nc.tensor, "PTrd")
            cT = self.inc(nc.tensor.transpose(PT[0:128, 0:128],
                                              penb16[:], idents[:]), "T")
            self.setev("penb16rd", "T", cT)
            self.w(nc.scalar, "T", cT)
            cA_pent = 0
            for t in range(NTIL):
                nc.scalar.drain()
                cA_pent = self.inc(nc.scalar.copy(
                    a_pent[:, t * 128:(t + 1) * 128],
                    PT[t * 32:t * 32 + 16, 0:128]), "A")
            self.setev("PTpen", "A", cA_pent)
            self.setev("PTrd", "A", cA_pent)
            return cA_pent

        def prep_route(cAx):
            for t in range(NTIL):
                route_tile(t, cAx[t])
            return route_fin()

        def phase_F(src_ap, dst_ap, tib, info=None, cAx=None, cA_pent=None, cW=None,
                    tail_load=None):
            if cW is None:
                cW = self.inc(gld(out=a_b2, in_=db2.unsqueeze(0)), "wt", 16)

            def loadln(g, t):
                return loadln_f(src_ap, tib, g, t)

            if info is None:
                info = []
                for t in range(NTIL):
                    i = tib[0] + t
                    slot = i % 5
                    qa, qb = (("in", "wt"), ("st", "in"),
                              ("wt", "st"), ("in", "wt"))[t]
                    src = src_ap[bi, tok0(0, t):tok0(0, t) + 128, :]
                    ca = load_tile(slot, src[:, 0:512], qa, half=0)
                    self.qfence(qb)
                    cb = self.inc(qeng[qb].dma_start(
                        out=xbuf[:, slot * D + 512:(slot + 1) * D],
                        in_=src[:, 512:1024]), qb, 16)
                    cpair = (ca, (qb, cb))
                    xns, cV = ln_tile(slot, t, cpair)
                    info.append((slot, xns, cV, cpair))
            slots = [info[t][0] for t in range(NTIL)]
            if cAx is None:
                cAx = [transposes_to_xnt(info[t][1], t, info[t][2])
                       for t in range(NTIL)]
            if cA_pent is None:
                cA_pent = prep_route(cAx)
            for g in range(NGRP):
                cA_h = []
                self.w(nc.tensor, "wt", cW)
                for c in range(HCH):
                    ph = PA[:, (c % 2) * 512:(c % 2 + 1) * 512]
                    if len(cA_h) >= 2:
                        self.w(nc.tensor, "A", cA_h[-2])
                    else:
                        self.wk(nc.tensor, ("PAh", c % 2))
                    for k in range(NCH):
                        nc.tensor.matmul(ph,
                                         w1s[:, k * HF + c * 128:k * HF + (c + 1) * 128],
                                         xnt[:, k * TG:(k + 1) * TG],
                                         start=(k == 0), stop=False)
                    if c == 0:
                        self.w(nc.tensor, "A", cA_pent)
                    ins = nc.tensor.matmul(ph, a_ind[:, (c % 8) * 128:(c % 8 + 1) * 128],
                                           a_pent[:, 0:TG],
                                           start=False, stop=True)
                    cT = self.inc(ins, "T")
                    self.w(nc.scalar, "T", cT)
                    nc.scalar.drain()
                    cA = self.inc(nc.scalar.activation(a_hT[:, c * TG:(c + 1) * TG],
                                                       ph, AF.Relu,
                                                       bias=b1s[:, c:c + 1]), "A")
                    cA_h.append(cA)
                    self.setev(("PAh", c % 2), "A", cA)
                nxt = []
                cAx2 = []
                for t in range(NTIL):
                    slot = slots[t]
                    py = PB if t % 2 == 0 else PC
                    self.w(nc.tensor, "A", cA_h[-1])
                    self.wk(nc.tensor, ("pyev", t % 2))
                    for half in range(2):
                        nc.tensor.matmul(py[:, half * 512:(half + 1) * 512], ones1[:],
                                         a_b2[0:1, half * 512:(half + 1) * 512],
                                         start=True, stop=False)
                    ins = None
                    for c in range(HCH):
                        for half in range(2):
                            ins = nc.tensor.matmul(
                                py[:, half * 512:(half + 1) * 512],
                                a_hT[:, c * TG + t * 128:c * TG + (t + 1) * 128],
                                w2s[:, c * D + half * 512:c * D + (half + 1) * 512],
                                start=False, stop=(c == HCH - 1))
                    cT = self.inc(ins, "T")
                    xa = xbuf[:, slot * D:(slot + 1) * D]
                    self.w(nc.vector, "T", cT)
                    wpairs(nc.vector, info[t][3])
                    nc.vector.drain()
                    cG = self.inc(nc.vector.scalar_tensor_tensor(
                        xa, py[:, 0:1024], 0.5, xa, OP.mult, OP.add), "V")
                    self.setev(("pyev", t % 2), "V", cG)
                    self.w(nc.scalar, "V", cG)
                    self.qfence("st")
                    cS = self.inc(nc.scalar.dma_start(
                        out=dst_ap[bi, tok0(g, t):tok0(g, t) + 128, :], in_=xa),
                        "st", 16)
                    self.slot_evt[slots[t]] = ("V", cG, "st", cS)
                    if g + 1 < NGRP:
                        for tt in ([0, 1] if t == 0 else
                                   [t + 1] if t + 1 < NTIL else []):
                            nxt.append(loadln(g + 1, tt))
                        while len(cAx2) < len(nxt) - 2:
                            tt = len(cAx2)
                            e = nxt[tt]
                            cAx2.append(transposes_to_xnt(e[1], tt, e[2]))
                            if len(cAx2) >= 2:
                                route_tile(len(cAx2) - 2, cAx2[-2])
                    elif tail_load is not None:
                        tail_load(t)
                if g + 1 < NGRP:
                    while len(cAx2) < NTIL:
                        tt = len(cAx2)
                        e = nxt[tt]
                        cAx2.append(transposes_to_xnt(e[1], tt, e[2]))
                        if len(cAx2) >= 2:
                            route_tile(len(cAx2) - 2, cAx2[-2])
                    route_tile(NTIL - 1, cAx2[NTIL - 1])
                    slots = [e[0] for e in nxt]
                    info = nxt
                    cA_pent = route_fin()
            tib[0] += NGRP * NTIL

        ftmp = xbuf[:, 4 * D:4 * D + 512]
        ftm2 = xbuf[:, 4 * D + 512:5 * D]

        def rope(dst_rot, pp, g, t, cT_proj):
            par = t % 2
            hv = lambda ap, fs, off: bass.AP(ap.tensor, ap.offset + off,
                                             [[fs, 128], [64, NH], [1, 32]])
            pq1 = bass.AP(pp, 0, [[1024, 128], [64, NH], [1, 32]])
            pq2 = bass.AP(pp, 32, [[1024, 128], [64, NH], [1, 32]])
            f1 = bass.AP(ftmp.tensor, ftmp.offset, [[5 * D, 128], [32, NH], [1, 32]])
            f2 = bass.AP(ftm2.tensor, ftm2.offset, [[5 * D, 128], [32, NH], [1, 32]])
            r1, r2 = hv(dst_rot, 2048, 0), hv(dst_rot, 2048, 32)
            cosap = bass.AP(coss, (g * NTIL + t) * 32, [[512, 128], [0, NH], [1, 32]])
            sinap = bass.AP(sins, (g * NTIL + t) * 32, [[512, 128], [0, NH], [1, 32]])
            self.w(nc.vector, "T", cT_proj)
            nc.vector.drain()
            nc.vector.tensor_tensor(f1, pq1, cosap, OP.mult)
            nc.vector.drain()
            nc.vector.tensor_tensor(f2, pq2, sinap, OP.mult)
            nc.vector.drain()
            nc.vector.tensor_sub(f1, f1, f2)
            self.wk(nc.vector, ("abufT", par))
            self.wk(nc.vector, ("abufst", par))
            nc.vector.drain()
            nc.vector.tensor_scalar_max(r1, f1, 0.0)
            nc.vector.drain()
            nc.vector.tensor_tensor(f1, pq1, sinap, OP.mult)
            nc.vector.drain()
            cVpp = self.inc(nc.vector.tensor_tensor(f2, pq2, cosap, OP.mult), "V")
            self.setev(("pp", 0 if pp is PA else 1), "V", cVpp)
            nc.vector.drain()
            nc.vector.tensor_add(f1, f1, f2)
            nc.vector.drain()
            cVr = self.inc(nc.vector.tensor_scalar_max(r2, f1, 0.0), "V")
            return cVr

        def loadln_q(g, t, q="in", slot=None):
            i = tib[0] + g * NTIL + t
            if slot is None:
                slot = i % NSLOT
            cIn = load_tile(slot, dxr[bi, tok0(g, t):tok0(g, t) + 128, :], q)
            xns, cV = ln_tile(slot, t, cIn)
            self.slot_evt[slot] = ("V", cV, "V", cV)
            return slot, xns, cV

        tib = [0]
        infoq = []
        phase_F(dx, dxr, tib,
                tail_load=lambda t: infoq.append(
                    loadln_q(0, t, q=("in", "wt", "in", "st")[t],
                             slot=(2, 3, 1, 0)[t])))

        cW = 0

        def a_proj(t, cin_or_cQx, cax=0):
            pp = PA if t % 2 == 0 else PB
            self.w(nc.tensor, "wt", cW)
            if cin_or_cQx:
                self.w(nc.tensor, "in", cin_or_cQx)
            if cax:
                self.w(nc.tensor, "A", cax)
            self.wk(nc.tensor, ("pp", t % 2))
            ins = None
            for k in range(NCH):
                for half in range(2):
                    ins = nc.tensor.matmul(
                        pp[:, half * 512:(half + 1) * 512],
                        xnt[:, k * TG + t * 128:k * TG + (t + 1) * 128],
                        a_wcur[:, k * D + half * 512:k * D + (half + 1) * 512],
                        start=(k == 0), stop=(k == NCH - 1))
            return pp, self.inc(ins, "T")

        def qtt_flush(pend):
            g, t, cVr = pend
            par = t % 2
            slot_r = abuf[:, par * 1024:(par + 1) * 1024]
            self.w(nc.tensor, "V", cVr)
            self.wk(nc.tensor, "PTrd")
            ins = None
            for c in range(NCH):
                ins = nc.tensor.transpose(PT[:, c * 128:(c + 1) * 128],
                                          slot_r[:, c * 128:(c + 1) * 128],
                                          idents[:])
            cT = self.inc(ins, "T")
            self.setev(("abufT", par), "T", cT)
            self.w(nc.scalar, "T", cT)
            nc.scalar.drain()
            cA = self.inc(nc.scalar.copy(slot_r, PT[:, 0:1024]), "A")
            self.setev("PTrd", "A", cA)
            self.w(nc.scalar, "A", cA)
            self.qfence("st")
            cS = self.inc(nc.scalar.dma_start(
                out=bass.AP(dqt.tensor, g * 128 * NCH * TG + t * 128,
                            [[NCH * TG, 128], [TG, NCH], [1, 128]]),
                in_=slot_r.rearrange("p (c f) -> p c f", c=NCH)), "st", 16)
            self.setev(("abufst", par), "st", cS)


        self.w(nc.gpsimd, "T", self.cnt["T"])
        cW = self.inc(gld(out=a_wcur, in_=chunk_ap(dwq, NCH, D)), "wt", 16)
        cAxq = [transposes_to_xnt(infoq[t][1], t, infoq[t][2]) for t in range(NTIL)]
        self.barrier()
        pendq = []
        for g in range(NGRP):
            cst = []
            for t in range(NTIL):
                self.qfence("in")
                self.w(nc.sync, "A", cAxq[t])
                cst.append(self.inc(
                    ld(out=bass.AP(dxnt.tensor, g * 128 * NCH * TG + t * 128,
                                   [[NCH * TG, 128], [TG, NCH], [1, 128]]),
                       in_=bass.AP(xnt, t * 128,
                                   [[NCH * TG, 128], [TG, NCH], [1, 128]])),
                    "in", 16))
            nxt = []
            cAx_next = []
            for t in range(NTIL):
                pp, cT = a_proj(t, 0, cax=cAxq[t])
                if len(pendq) >= 2:
                    qtt_flush(pendq.pop(0))
                cVr = rope(abuf[:, (t % 2) * 1024:(t % 2 + 1) * 1024], pp, g, t, cT)
                pendq.append((g, t, cVr))
                if g + 1 < NGRP:
                    nxt.append(loadln_q(g + 1, t))
                    if t >= 1:
                        e = nxt[t - 1]
                        self.w(nc.tensor, "in", cst[t - 1])
                        cAx_next.append(transposes_to_xnt(e[1], t - 1, e[2]))
            if g + 1 < NGRP:
                e = nxt[NTIL - 1]
                self.w(nc.tensor, "in", cst[NTIL - 1])
                cAx_next.append(transposes_to_xnt(e[1], NTIL - 1, e[2]))
                cAxq = cAx_next
                infoq = nxt
        while pendq:
            qtt_flush(pendq.pop(0))
        tib[0] += NGRP * NTIL
        self.w(nc.gpsimd, "T", self.cnt["T"])
        cW_next = self.inc(gld(out=a_wcur, in_=chunk_ap(dwk, NCH, D)), "wt", 16)
        self.barrier()
        self.ev.pop(("pp", 0), None)
        self.ev.pop(("pp", 1), None)

        cW = cW_next
        def xnt_tile_ap(g, tt):
            return (bass.AP(xnt, tt * 128, [[NCH * TG, 128], [TG, NCH], [1, 128]]),
                    bass.AP(dxnt.tensor, g * 128 * NCH * TG + tt * 128,
                            [[NCH * TG, 128], [TG, NCH], [1, 128]]))

        def ksum_flush(gp, tp, cVrp):
            par = tp % 2
            slot_rp = abuf[:, par * 1024:(par + 1) * 1024]
            self.w(nc.tensor, "V", cVrp)
            ti = gp * NTIL + tp
            ins = None
            for half in range(2):
                ins = nc.tensor.matmul(
                    PC[64:65, half * 512:(half + 1) * 512], onesc[:],
                    slot_rp[:, half * 512:(half + 1) * 512],
                    start=(ti == 0), stop=(ti == NGRP * NTIL - 1))
            cTk = self.inc(ins, "T")
            self.setev(("abufT", par), "T", cTk)
            self.w(nc.scalar, "V", cVrp)
            self.qfence("st")
            cS = self.inc(nc.scalar.dma_start(
                out=dkr[tok0(gp, tp):tok0(gp, tp) + 128, :], in_=slot_rp), "st", 16)
            self.setev(("abufst", par), "st", cS)

        ksq = []
        cT_projk = [0, 0, 0, 0]
        for g in range(NGRP):
            self.qfence("in")
            self.w(nc.sync, "T")
            cQx_g = self.inc(ld(out=xnt[:], in_=dxnt[g]), "in", 16)
            for t in range(NTIL):
                pp, cT = a_proj(t, cQx_g)
                if g == NGRP - 1:
                    cT_projk[t] = cT
                if len(ksq) >= 2:
                    ksum_flush(*ksq.pop(0))
                cVr = rope(abuf[:, (t % 2) * 1024:(t % 2 + 1) * 1024], pp, g, t, cT)
                cT_lastproj = cT
                ksq.append((g, t, cVr))
        while ksq:
            ksum_flush(*ksq.pop(0))
        self.w(nc.gpsimd, "T", cT_lastproj)
        cW_next = self.inc(gld(out=a_wcur, in_=chunk_ap(dwv, NCH, D)), "wt", 16)
        self.qfence("in")
        self.w(nc.sync, "T", cT_projk[NTIL - 1])
        cQxs0 = [("in", self.inc(ld(out=xnt[:], in_=dxnt[0]), "in", 16))] * NTIL
        self.qfence("in")
        cKpre = self.inc(ld(out=xn[:, 2 * D:3 * D], in_=dkr[0:128, :]), "in", 16)
        self.barrier()
        self.ev.pop(("pp", 0), None)
        self.ev.pop(("pp", 1), None)

        cW = cW_next
        last = NGRP * NTIL - 1

        def kv_mm(g, t, cA_v, cK):
            vs = xn[:, (t % 2) * D:(t % 2 + 1) * D]
            ks = xn[:, (2 + t % 2) * D:(3 + t % 2) * D]
            self.w(nc.tensor, "A", cA_v)
            self.w(nc.tensor, "in", cK)
            ti = g * NTIL + t
            ins = None
            for h in range(NH):
                ins = nc.tensor.matmul(PC[0:64, h * 64:(h + 1) * 64],
                                       vs[:, h * 64:(h + 1) * 64],
                                       ks[:, h * 64:(h + 1) * 64],
                                       start=(ti == 0 and h % 8 == 0),
                                       stop=(ti == last and h % 8 == 7))
            cT = self.inc(ins, "T")
            self.setev(("kvv", t % 2), "T", cT)
            self.setev(("kvk", t % 2), "T", cT)

        pendkv = None
        cKnext = [0, 0]
        for g in range(NGRP):
            if g == 0:
                cQxs = cQxs0
            else:
                self.qfence("in")
                self.w(nc.sync, "T")
                cQxs = [("in", self.inc(ld(out=xnt[:], in_=dxnt[g]), "in", 16))] * NTIL
            for t in range(NTIL):
                ti_g = g * NTIL + t
                cK = cKpre if ti_g == 0 else cKnext[ti_g % 2]
                pp = PA if t % 2 == 0 else PB
                self.w(nc.tensor, "wt", cW)
                self.w(nc.tensor, cQxs[t][0], cQxs[t][1])
                self.wk(nc.tensor, ("pp", t % 2))
                ins = None
                for k in range(NCH):
                    for half in range(2):
                        ins = nc.tensor.matmul(
                            pp[:, half * 512:(half + 1) * 512],
                            xnt[:, k * TG + t * 128:k * TG + (t + 1) * 128],
                            a_wcur[:, k * D + half * 512:k * D + (half + 1) * 512],
                            start=(k == 0), stop=(k == NCH - 1))
                cT = self.inc(ins, "T")
                vs = xn[:, (t % 2) * D:(t % 2 + 1) * D]
                self.w(nc.scalar, "T", cT)
                self.wk(nc.scalar, ("kvv", t % 2))
                nc.scalar.drain()
                cA = self.inc(nc.scalar.activation(vs, pp[:, 0:1024], AF.Copy), "A")
                self.setev(("pp", t % 2), "A", cA)
                if pendkv is not None:
                    kv_mm(*pendkv)
                if ti_g + 1 <= last:
                    tn = ti_g + 1
                    gn, ttn = tn // NTIL, tn % NTIL
                    ksn = xn[:, (2 + tn % 2) * D:(3 + tn % 2) * D]
                    self.qfence("in")
                    self.wk(nc.sync, ("kvk", tn % 2))
                    cKnext[tn % 2] = self.inc(
                        ld(out=ksn, in_=dkr[tok0(gn, ttn):tok0(gn, ttn) + 128, :]),
                        "in", 16)
                pendkv = (g, t, cA, cK)
        self.w(nc.gpsimd, "T", self.cnt["T"])
        cW_wo = self.inc(gld(out=a_wo, in_=chunk_ap(dwo, NCH, D)), "wt", 16)
        kv_mm(*pendkv)
        self.w(nc.scalar, "T")
        nc.scalar.drain()
        cA_kv = self.inc(nc.scalar.activation(abuf[0:65, 0:1024], PC[0:65, 0:1024],
                                              AF.Copy), "A")
        nc.scalar.drain()
        cA_kv = self.inc(nc.scalar.activation(abuf[64:128, 1024:2048],
                                              PC[0:64, 0:1024], AF.Copy), "A")
        self.w(nc.vector, "A", cA_kv)
        nc.vector.drain()
        cV = self.inc(nc.vector.memset(bds[:], 0.0), "V")
        self.w(nc.tensor, "A", cA_kv)
        self.w(nc.tensor, "V", cV)
        ins = None
        for c in range(NCH):
            ins = nc.tensor.matmul(PL[:, c:c + 1],
                                   abuf[64:65, c * 128:(c + 1) * 128],
                                   onesc[64:65, 0:1], start=True, stop=True)
        cT = self.inc(ins, "T")
        self.w(nc.scalar, "T", cT)
        cA_bd = 0
        for c in range(NCH):
            for hh in range(2):
                h = 2 * c + hh
                nc.scalar.drain()
                cA_bd = self.inc(nc.scalar.copy(
                    bds[hh * 64:(hh + 1) * 64, c * NH + h:c * NH + h + 1],
                    PL[hh * 64:(hh + 1) * 64, c:c + 1]), "A")
        self.barrier()

        cA_m = 0
        for c in range(NCH):
            pm = PA if c % 2 == 0 else PC
            self.w(nc.tensor, "wt", cW_wo)
            if c == 1:
                self.w(nc.tensor, "A", cA_kv)
            self.wk(nc.tensor, ("PAm", c % 2))
            ins = None
            for hh in range(2):
                h = 2 * c + hh
                kvsl = (abuf[0:64, h * 64:(h + 1) * 64] if h % 2 == 0 else
                        abuf[64:128, 1024 + h * 64:1024 + (h + 1) * 64])
                for half in range(2):
                    ins = nc.tensor.matmul(
                        pm[hh * 64:(hh + 1) * 64, half * 512:(half + 1) * 512],
                        kvsl,
                        a_wo[(h % 2) * 64:(h % 2) * 64 + 64,
                             (h // 2) * 1024 + half * 512:(h // 2) * 1024 + (half + 1) * 512],
                        start=True, stop=True)
            cT = self.inc(ins, "T")
            self.setev("kvread", "T", cT)
            self.w(nc.scalar, "T", cT)
            nc.scalar.drain()
            cA_m = self.inc(nc.scalar.copy(a_m[:, c * 1024:(c + 1) * 1024],
                                           pm[:, 0:1024]), "A")
            self.setev(("PAm", c % 2), "A", cA_m)
        def b_loads(lc):
            self.qfence("in")
            self.wk(nc.sync, "qsread")
            cQ = self.inc(ld(out=xnt[:], in_=dqt[lc]), "in", 16)
            return [("in", cQ)]

        def b_den(lc, cQ):
            for qq, c in cQ:
                self.w(nc.tensor, qq, c)
            self.w(nc.tensor, "A", cA_bd)
            self.wk(nc.tensor, "PLden")
            ins = None
            for c in range(NCH):
                ins = nc.tensor.matmul(PL[0:16, 0:512],
                                       bds[:, c * NH:(c + 1) * NH],
                                       xnt[:, c * TG:(c + 1) * TG],
                                       start=(c == 0), stop=(c == NCH - 1))
            cT = self.inc(ins, "T")
            self.w(nc.vector, "T", cT)
            self.w(nc.vector, "A", cA_m)
            nc.vector.drain()
            nc.vector.tensor_scalar(PL[0:16, 0:512], PL[0:16, 0:512], EPS_AT,
                                    None, OP.add)
            nc.vector.drain()
            with nc.allow_low_precision(reason="recip_den bf16 matmul operand"):
                cV_r = self.inc(nc.vector.reciprocal(a_rd[:, lc * TG:(lc + 1) * TG],
                                                     PL[0:16, 0:512]), "V")
            self.setev("PLden", "V", cV_r)
            return cV_r

        def b_bc_qscale(lc, cV_r):
            cV_qs = 0
            for c in range(NCH):
                pbh = PB[:, (c % 2) * 512:(c % 2 + 1) * 512]
                self.wk(nc.tensor, ("pbh", c % 2))
                self.w(nc.tensor, "V", cV_r)
                cT = self.inc(nc.tensor.matmul(pbh, a_ind[:, c * 128:(c + 1) * 128],
                                               a_rd[:, lc * TG:(lc + 1) * TG],
                                               start=True, stop=True), "T")
                self.w(nc.vector, "T", cT)
                self.wk(nc.vector, ("qsrd", c))
                nc.vector.drain()
                cV_qs = self.inc(nc.vector.tensor_tensor(
                    xn[:, c * TG:(c + 1) * TG], xnt[:, c * TG:(c + 1) * TG],
                    pbh, OP.mult), "V")
                self.setev(("pbh", c % 2), "V", cV_qs)
            self.setev("qsread", "V", cV_qs)
            return cV_qs

        cQn = b_loads(0)
        cV_rn = b_den(0, cQn)
        cS_b0 = []
        infoF2t = []

        def b_tail_load(t):
            slot = (3, 4, 0, 1)[t]
            cIn = load_tile(slot, dxr[bi, tok0(0, t):tok0(0, t) + 128, :],
                            ("wt", "in", "wt", "st")[t], extra=("st", cS_b0[t]))
            self.wk(nc.scalar, "kvread")
            cA1 = ln_heavy(slot, t, cIn,
                           sqout=abuf[:, (t % 2) * 1024:(t % 2 + 1) * 1024])
            self.w(nc.vector, "A", cA1)
            cA2 = ln_smalls([t])
            infoF2t.append((slot, cA2, cIn))

        for lc in range(NGRP):
            cV_qs = b_bc_qscale(lc, cV_rn)
            for t in range(NTIL):
                i = tib[0] + lc * NTIL + t
                cIn = load_tile(i % 5, dxr[bi, tok0(lc, t):tok0(lc, t) + 128, :])
                pao = PA if t % 2 == 0 else PC
                self.w(nc.tensor, "V", cV_qs)
                self.w(nc.tensor, "A", cA_m)
                self.wk(nc.tensor, ("paoev", t % 2))
                ins = None
                for c in range(NCH):
                    for half in range(2):
                        ins = nc.tensor.matmul(
                            pao[:, half * 512:(half + 1) * 512],
                            xn[:, c * TG + t * 128:c * TG + (t + 1) * 128],
                            a_m[:, c * 1024 + half * 512:c * 1024 + (half + 1) * 512],
                            start=(c == 0), stop=(c == NCH - 1))
                cT = self.inc(ins, "T")
                if t == NTIL - 1:
                    for c in range(NCH):
                        self.setev(("qsrd", c), "T", cT)
                xa = xbuf[:, (i % 5) * D:((i % 5) + 1) * D]
                self.w(nc.vector, "T", cT)
                self.w(nc.vector, cIn[0], cIn[1])
                nc.vector.drain()
                cG = self.inc(nc.vector.tensor_add(xa, xa, pao[:, 0:1024]), "V")
                self.setev(("paoev", t % 2), "V", cG)
                self.w(nc.scalar, "V", cG)
                self.qfence("st")
                cS = self.inc(nc.scalar.dma_start(
                    out=dxr[bi, tok0(lc, t):tok0(lc, t) + 128, :], in_=xa), "st", 16)
                if lc == 0:
                    cS_b0.append(cS)
                self.slot_evt[i % 5] = ("V", cG, "st", cS)
                if lc + 1 < NGRP:
                    if t == 0:
                        cQn = b_loads(lc + 1)
                    elif t == 1:
                        cV_rn = b_den(lc + 1, cQn)
                else:
                    cT_attn_last = cT
                    b_tail_load(t)
        tib[0] += NGRP * NTIL
        for t in range(NTIL):
            self.ev[("xnrd", t)] = ("T", cT_attn_last)
        infoF2 = []
        cAxF2 = []
        for t in range(NTIL):
            slot, cA2, cIn = infoF2t[t]
            xns, cV = ln_xn(slot, t, cA2)
            infoF2.append((slot, xns, cV, cIn))
            cAxF2.append(transposes_to_xnt(xns, t, cV))
        self.ev[("PLlog", 0)] = self.ev["PLden"]
        self.ev[("PLlog", 1)] = self.ev[("paoev", 0)]
        self.ev[("PLlog", 2)] = self.ev[("paoev", 0)]
        self.ev[("PLlog", 3)] = self.ev["PLden"]
        cA_pentF2 = prep_route(cAxF2)
        self.barrier()

        phase_F(dxr, dout, tib, infoF2, cAxF2, cA_pentF2, cW=0)
        self.barrier()

        for s in self.sem:
            nc.regs_alu(regs[s], regs[s], self.cnt[s], op=OP.add)
        fori_cm.__exit__(None, None, None)
        es.close()
        return nc


def _host_prep(inputs):
    perm = np.zeros(D, np.int64)
    for h in range(NH):
        for j in range(32):
            perm[h * 64 + j] = h * 64 + 2 * j
            perm[h * 64 + 32 + j] = h * 64 + 2 * j + 1
    freqs = 1.0 / 10000.0 ** (np.arange(0, 64, 2, dtype=np.float64) / 64)
    ang = np.arange(S, dtype=np.float64)[:, None] * freqs[None, :]
    cos64 = np.cos(ang).reshape(16, 128, 32)
    sin64 = np.sin(ang).reshape(16, 128, 32)
    indc = np.zeros((NH, NCH * 128), np.float32)
    for c in range(NCH):
        indc[2 * c, c * 128:c * 128 + 64] = 1.0
        indc[2 * c + 1, c * 128 + 64:(c + 1) * 128] = 1.0
    psi = np.zeros(HF, np.int64)
    for c in range(HCH):
        el = 2 * (c % 8)
        psi[c * 128:c * 128 + 64] = el * 256 + (c // 8) * 64 + np.arange(64)
        psi[c * 128 + 64:(c + 1) * 128] = (el + 1) * 256 + (c // 8) * 64 + np.arange(64)
    return {
        "w1t": np.ascontiguousarray(np.asarray(inputs["L0_ffn_w1"]).T[:, psi]),
        "w2t": np.ascontiguousarray(np.asarray(inputs["L0_ffn_w2"]).T[psi, :]),
        "wq": np.ascontiguousarray(np.asarray(inputs["L0_attn_Wq"])[:, perm]),
        "wk": np.ascontiguousarray(np.asarray(inputs["L0_attn_Wk"])[:, perm]),
        "wv": np.ascontiguousarray(np.asarray(inputs["L0_attn_Wv"])),
        "wo": np.ascontiguousarray(np.asarray(inputs["L0_attn_Wo"])),
        "rwt": np.ascontiguousarray(np.asarray(inputs["L0_ffn_router_w"]).T),
        "rb": np.ascontiguousarray(np.asarray(inputs["L0_ffn_router_b"])),
        "b1": np.ascontiguousarray(np.asarray(inputs["L0_ffn_b1"])[psi]),
        "b2": np.ascontiguousarray(np.asarray(inputs["L0_ffn_b2"])),
        "ident": np.eye(128, dtype=np.float32).astype(BF),
        "cost": cos64.astype(np.float32),
        "sint": sin64.astype(np.float32),
        "costl": (cos64 - cos64.astype(np.float32).astype(np.float64)).astype(np.float32),
        "sintl": (sin64 - sin64.astype(np.float32).astype(np.float64)).astype(np.float32),
        "indc": indc.astype(BF),
    }


def kernel(**inputs):
    x = np.asarray(inputs["L0_attn_x"])
    common = _host_prep(inputs)
    nc = _B().build()
    in_maps = []
    for i in range(8):
        m = dict(common)
        m["x"] = np.ascontiguousarray(x[i * NB:(i + 1) * NB])
        in_maps.append(m)
    res = run_bass_kernel_spmd(nc, in_maps, core_ids=list(range(8)))
    return np.concatenate([r["out"] for r in res.results], axis=0)
